# version 04 (final)
# baseline (speedup 1.0000x reference)
import numpy as np
import ml_dtypes
from contextlib import ExitStack

import concourse.bass as bass
import concourse.mybir as mybir
from concourse.bass_utils import run_bass_kernel_spmd

F32 = mybir.dt.float32
BF16 = mybir.dt.bfloat16
ALU = mybir.AluOpType
ACT = mybir.ActivationFunctionType
AX = mybir.AxisListType
BF = ml_dtypes.bfloat16

NTOK = 4096
D = 1024
NH = 4
DQK = 256
DV = 512
DVT = 2048
IFN = 2816
CT = 256
NTILE = NTOK // CT
QB = 16
ROPE_BASE = 10000.0
EPS = 1e-5
NCORES = 8

SCRATCH_KIND = "ExternalOutput"
DEBUG_TAP = False
ENGS = ("sync", "tensor", "vector", "scalar", "gpsimd")


def _consts():
    gam = 1.0 - np.exp2(-5.0 - np.arange(NH, dtype=np.float64))
    c = {}
    inv_freq = 1.0 / ROPE_BASE ** (np.arange(0, DQK, 2, dtype=np.float64) / DQK)
    fr = np.outer(np.arange(NTOK, dtype=np.float64), inv_freq)
    c["cosF"] = np.ascontiguousarray(np.cos(fr).T.astype(np.float32))
    c["sinF"] = np.ascontiguousarray(np.sin(fr).T.astype(np.float32))
    i = np.arange(128)
    rel = i[None, :] - i[:, None]
    dm = np.zeros((128, NH * 2 * 128), dtype=np.float64)
    for h in range(NH):
        g = gam[h]
        diag = np.where(rel >= 0, g ** np.maximum(rel, 0) / 16.0, 0.0)
        off = g ** (rel + 128.0) / 16.0
        dm[:, (h * 2) * 128:(h * 2 + 1) * 128] = diag
        dm[:, (h * 2 + 1) * 128:(h * 2 + 2) * 128] = off
    c["dmask"] = np.ascontiguousarray(dm.astype(np.float32))
    idx = np.arange(CT, dtype=np.float64)
    qs = np.concatenate([gam[h] ** (idx + 1) / 16.0 for h in range(NH)])
    c["qsF"] = np.ascontiguousarray(
        np.broadcast_to(qs[None, :], (128, NH * CT)).astype(np.float32))
    ks = np.zeros((128, 2 * NH), dtype=np.float32)
    for u in range(2):
        for h in range(NH):
            p = u * 128 + np.arange(128)
            ks[:, u * NH + h] = gam[h] ** (CT - 1 - p)
    c["ks_pp"] = ks
    c["ident"] = np.eye(128, dtype=BF)
    c["gammaC"] = [float(g ** CT) for g in gam]
    return c


class Sem:
    __slots__ = ("h", "n")

    def __init__(self, h):
        self.h = h
        self.n = 0

    def inc(self, k=1):
        self.n += k
        return (self.h, self.n)

    def cur(self):
        return (self.h, self.n) if self.n else None


class Prog:
    def __init__(self):
        self.ops = {e: [] for e in ENGS}
        self.sem_by_handle = {}

    def op(self, eng, meth, *args, sem=None, k=1, **kw):
        tag = sem.inc(k) if sem is not None else None
        self.ops[eng].append(
            ("op", meth, args, kw, sem.h if sem is not None else None, k))
        return tag

    def wait(self, eng, tag):
        if tag is not None:
            self.ops[eng].append(("wait", tag[0], tag[1]))

    def mark(self, eng, sem, k=1):
        for i in range(len(self.ops[eng]) - 1, -1, -1):
            it = self.ops[eng][i]
            if it[0] != "op":
                continue
            if it[4] is None:
                self.ops[eng][i] = ("op", it[1], it[2], it[3], sem.h, k)
                return sem.inc(k)
            prev_sem = self.sem_by_handle[it[4]]
            self.ops[eng].append(("wait", prev_sem.h, prev_sem.n))
            self.ops[eng].append(("op", "nop", (), {}, sem.h, k))
            return sem.inc(k)
        self.ops[eng].append(("op", "nop", (), {}, sem.h, k))
        return sem.inc(k)

    def replay(self, name, eng):
        for it in self.ops[name]:
            if it[0] == "wait":
                eng.wait_ge(it[1], it[2])
            else:
                _, meth, args, kw, semh, k = it
                ins = getattr(eng, meth)(*args, **kw)
                if semh is not None:
                    ins.then_inc(semh, k)


def emit_rsqrt(P, s_v, w_ap, y_ap, t_ap, lad_ap, kcols):
    def fence():
        P.wait("vector", P.mark("vector", s_v))

    fence()
    lad3 = lad_ap.rearrange("p (b j) -> p b j", j=16)
    P.op("vector", "tensor_scalar", lad3[:, :, 15], w_ap, 0.0,
         float(2.0 ** 10 * 0.7071067811865476), ALU.mult, ALU.add)
    for i, k in enumerate(range(-9, 6)):
        P.op("vector", "tensor_scalar", lad3[:, :, i], w_ap,
             float(4.0 ** k), float(-(2.0 ** -k) * 0.7071067811865476),
             ALU.is_gt, ALU.mult)
    fence()
    P.op("vector", "reduce_sum", y_ap, lad3, AX.X)
    fence()
    for _ in range(7):
        P.op("vector", "tensor_tensor", t_ap, y_ap, y_ap, ALU.mult)
        fence()
        P.op("vector", "scalar_tensor_tensor", t_ap, t_ap, -0.5, w_ap,
             ALU.mult, ALU.mult)
        fence()
        P.op("vector", "tensor_scalar", t_ap, t_ap, 1.0, 1.5,
             ALU.mult, ALU.add)
        fence()
        P.op("vector", "tensor_tensor", y_ap, y_ap, t_ap, ALU.mult)
        fence()


def build_nc():
    nc = bass.Bass("TRN2", target_bir_lowering=False,
                   detect_race_conditions=False)
    C = _consts()
    gammaC = C["gammaC"]

    x_d = nc.dram_tensor("x", [NTOK, D], F32, kind="ExternalInput").ap()
    wq_d = nc.dram_tensor("Wq", [D, D], F32, kind="ExternalInput").ap()
    wk_d = nc.dram_tensor("Wk", [D, D], F32, kind="ExternalInput").ap()
    wv_d = nc.dram_tensor("Wv", [D, DVT], F32, kind="ExternalInput").ap()
    wg_d = nc.dram_tensor("Wg", [D, DVT], F32, kind="ExternalInput").ap()
    wo_d = nc.dram_tensor("Wo", [DVT, D], F32, kind="ExternalInput").ap()
    gnr_d = nc.dram_tensor("gnr", [128, DV], F32, kind="ExternalInput").ap()
    fg_d = nc.dram_tensor("fwg", [IFN, D], F32, kind="ExternalInput").ap()
    fu_d = nc.dram_tensor("fwu", [IFN, D], F32, kind="ExternalInput").ap()
    fd_d = nc.dram_tensor("fwd", [D, IFN], F32, kind="ExternalInput").ap()
    out_d = nc.dram_tensor("out", [NTOK, D], F32, kind="ExternalOutput").ap()

    dbg_d = (nc.dram_tensor("dbg", [128, 8 * D], BF16, kind="ExternalOutput").ap()
             if DEBUG_TAP else None)
    dbgf_d = (nc.dram_tensor("dbgf", [128, 2 * D + 64], F32, kind="ExternalOutput").ap()
              if DEBUG_TAP else None)
    gated_s = nc.dram_tensor("gated_s", [NTOK, DVT], BF16, kind=SCRATCH_KIND).ap()
    x2_s = nc.dram_tensor("x2_s", [NTOK, D], BF16, kind=SCRATCH_KIND).ap()
    wgq_s = nc.dram_tensor("wgq_s", [IFN, D], BF16, kind=SCRATCH_KIND).ap()
    wuq_s = nc.dram_tensor("wuq_s", [IFN, D], BF16, kind=SCRATCH_KIND).ap()
    wdq_s = nc.dram_tensor("wdq_s", [D, IFN], BF16, kind=SCRATCH_KIND).ap()

    cosF_d = nc.inline_tensor(C["cosF"], name="cosFc").ap()
    sinF_d = nc.inline_tensor(C["sinF"], name="sinFc").ap()
    dmask_d = nc.inline_tensor(C["dmask"], name="dmaskc").ap()
    qsF_d = nc.inline_tensor(C["qsF"], name="qsFc").ap()
    kspp_d = nc.inline_tensor(C["ks_pp"], name="ksppc").ap()
    ident_d = nc.inline_tensor(C["ident"], name="identc").ap()
    thr_np = np.broadcast_to(np.array(
        [-0.25, -0.75, -1.25, -1.75, -2.5, -3.5, -5.0], np.float32)[None, :],
        (128, 7)).copy()
    thr_d = nc.inline_tensor(thr_np, name="thr7c").ap()
    zeros_d = nc.inline_tensor(np.zeros((128, DV), np.float32), name="zeroc").ap()

    es = ExitStack()

    def sb(name, shape, dt):
        return es.enter_context(nc.sbuf_tensor(name, shape, dt))

    def sem(name):
        sm = Sem(es.enter_context(nc.semaphore(name)))
        P.sem_by_handle[sm.h] = sm
        return sm

    P = Prog()

    with es:
        ident = sb("ident_sb", [128, 128], BF16)

        s_w = sem("s_w")
        s_v = sem("s_v")
        s_t = sem("s_t")
        s_sc = sem("s_sc")
        s_g = sem("s_g")
        s_st = sem("s_st")
        s_ph = sem("s_ph")

        s_wc = sem("s_wc")

        def wcast(dst, src, rows):
            nch = rows // 128
            P.op("gpsimd", "dma_start", sem=s_wc, k=16,
                 out=dst[:].rearrange("p (c d) -> p c d", c=nch),
                 in_=src.rearrange("(c p) d -> p c d", p=128))

        def barrier(tag_in=None):
            base = s_ph.n
            for e in ENGS:
                P.mark(e, s_ph)
            for e in ENGS:
                P.wait(e, (s_ph.h, base + len(ENGS)))

        with ExitStack() as e1:
            def sb1(n, s, d):
                return e1.enter_context(nc.sbuf_tensor(n, s, d))

            def ps1(n, s, d=F32):
                return e1.enter_context(nc.psum_tensor(n, s, d))

            w_q = sb1("w_q", [128, 8 * D], BF16)
            w_k = sb1("w_k", [128, 8 * D], BF16)
            w_v = sb1("w_v", [128, 8 * DVT], BF16)
            w_g = sb1("w_g", [128, 8 * DVT], BF16)
            gnr = sb1("gnr_sb", [128, DV], F32)
            S_st = sb1("S_st", [128, 8 * DV], F32)
            dmask = sb1("dmask_sb", [128, NH * 2 * 128], F32)
            qsF = sb1("qsF_sb", [128, NH * CT], F32)
            ks_pp = sb1("ks_pp_sb", [128, 2 * NH], F32)
            thr7 = sb1("thr7_sb", [128, 7], F32)
            wcast(w_q, wq_d, D)
            wcast(w_k, wk_d, D)
            wcast(w_v, wv_d, D)
            wcast(w_g, wg_d, D)
            w_cast_done = s_wc.cur()
            for dst, src in ((gnr, gnr_d), (ident, ident_d), (dmask, dmask_d),
                             (qsF, qsF_d), (ks_pp, kspp_d), (thr7, thr_d)):
                P.op("sync", "dma_start", sem=s_w, k=16, out=dst[:], in_=src)
            for i in range(8):
                P.op("sync", "dma_start", sem=s_w, k=16,
                     out=S_st[:, i * DV:(i + 1) * DV], in_=zeros_d)
            consts_done = s_w.cur()
            t_zero = consts_done

            x_bf = sb1("x_bf", [128, 2 * D], BF16)
            xT = sb1("xT", [128, 8 * CT], BF16)
            q_fm = sb1("q_fm", [128, 8 * CT], BF16)
            k_fm = sb1("k_fm", [128, 8 * CT], BF16)
            kck = sb1("kck", [128, 2 * D], BF16)
            v_tm = sb1("v_tm", [128, 2 * DVT], BF16)
            g_si = sb1("g_si", [128, 2 * DVT], BF16)
            o_sb = sb1("o_sb", [128, 2 * DVT], BF16)
            mT = sb1("mT", [128, 3 * 128], BF16)
            Sg = sb1("Sg", [128, 2 * DV], BF16)
            rs_s = sb1("rs_s", [128, 8], F32)
            rs_w = sb1("rs_w", [128, 8], F32)
            rs_y = sb1("rs_y", [128, 8], F32)
            rs_t = sb1("rs_t", [128, 8], F32)
            rs_lad = sb1("rs_lad", [128, 8 * 16], F32)
            sqj = sb1("sqj", [128, DV], BF16)
            u1 = sb1("u1", [128, CT], BF16)
            u2 = sb1("u2", [128, CT], BF16)
            u3 = sb1("u3", [128, CT], BF16)
            u4 = sb1("u4", [128, CT], BF16)
            u5 = sb1("u5", [128, CT], BF16)
            u6 = sb1("u6", [128, CT], BF16)
            cosb = sb1("cosb", [128, 2 * CT], BF16)
            sinb = sb1("sinb", [128, 2 * CT], BF16)
            csf32 = sb1("csf32", [128, 2 * 2 * CT], F32)
            wf32 = sb1("wf32", [128, 2 * D], F32)
            nfq = sb1("nfq", [128, D], F32)
            ca = sb1("ca", [128, D], BF16)
            cb = sb1("cb", [128, D], BF16)
            t1q = sb1("t1q", [128, D], BF16)
            t2q = sb1("t2q", [128, D], BF16)
            t3q = sb1("t3q", [128, D], BF16)
            sgq = sb1("sgq", [128, D], BF16)
            wqo = sb1("wqo", [128, D], BF16)
            amq = sb1("amq", [128, D // QB], F32)
            invq = sb1("invq", [128, D // QB], F32)
            shq = sb1("shq", [128, D // QB], F32)

            pt_a = ps1("pt_a", [128, 1024], BF16)
            pp_a = ps1("pp_a", [128, 512])
            pp_b = ps1("pp_b", [128, 512])
            p_sc = ps1("p_sc", [128, 384])
            p_o0 = ps1("p_o0", [128, 512])
            p_o1 = ps1("p_o1", [128, 512])
            p_s0 = ps1("p_s0", [128, 512])
            p_s1 = ps1("p_s1", [128, 512])

            s_x = sem("s_x")
            s_rp = sem("s_rp")
            s_wl = sem("s_wl")
            s_wv = sem("s_wv")
            s_ws = sem("s_ws")
            s_wst = sem("s_wst")

            wd_groups = [(0, 1024), (1024, 1024), (2048, 768)]
            wjobs = ([("g", ic, 0, D) for ic in range(IFN // 128)]
                     + [("u", ic, 0, D) for ic in range(IFN // 128)]
                     + [("d", fc, c0, cw) for fc in range(D // 128)
                        for (c0, cw) in wd_groups])
            wsrc = {"g": fg_d, "u": fu_d, "d": fd_d}
            wdst = {"g": wgq_s, "u": wuq_s, "d": wdq_s}
            wload_tags = {}
            wdone_tags = {}
            wstore_tags = {}

            def emit_wload(j):
                mat, ch, c0, cw = wjobs[j]
                buf = j % 2
                if j >= 2:
                    P.wait("scalar", wdone_tags[j - 2])
                src = wsrc[mat][ch * 128:(ch + 1) * 128, c0:c0 + cw]
                P.op("scalar", "dma_start", sem=s_wl, k=16,
                     out=wf32[:, buf * D:buf * D + cw], in_=src)
                wload_tags[j] = s_wl.cur()

            def emit_wquant(j):
                mat, ch, c0, cw = wjobs[j]
                buf = j % 2
                nb = cw // QB
                wv_ = wf32[:, buf * D:buf * D + cw]
                wv3 = wv_.rearrange("p (b k) -> p b k", k=QB)
                P.wait("vector", wload_tags[j])
                def wvf():
                    P.wait("vector", P.mark("vector", s_wv))
                P.op("vector", "reduce_max", amq[:, :nb], wv3, AX.X, True)
                wvf()
                P.op("vector", "tensor_scalar", invq[:, :nb], amq[:, :nb],
                     0.1666666716337204, None, ALU.mult)
                wvf()
                P.op("vector", "scalar_tensor_tensor", invq[:, :nb],
                     amq[:, :nb], -4.967053740534411e-09, invq[:, :nb], ALU.mult, ALU.add)
                wvf()
                P.op("vector", "tensor_scalar", invq[:, :nb], invq[:, :nb],
                     1.0, 1e-12, ALU.mult, ALU.max)
                wvf()
                P.op("vector", "tensor_scalar", shq[:, :nb], invq[:, :nb],
                     0.5, None, ALU.mult)
                wvf()
                P.op("vector", "reciprocal", invq[:, :nb], invq[:, :nb])
                wvf()
                inv_b = invq[:, :nb].unsqueeze(2).broadcast_to([128, nb, QB])
                sh_b = shq[:, :nb].unsqueeze(2).broadcast_to([128, nb, QB])
                nf3 = nfq[:, :cw].rearrange("p (b k) -> p b k", k=QB)
                P.op("vector", "tensor_tensor", nf3, wv3, inv_b, ALU.mult)
                tagn = P.op("vector", "scalar_tensor_tensor", nfq[:, :cw],
                            nfq[:, :cw], -1.0, nfq[:, :cw], ALU.mult, ALU.max,
                            sem=s_wv)
                a_ap = nfq[:, :cw]
                P.wait("scalar", wload_tags[j])
                if j >= 1:
                    P.wait("scalar", wdone_tags[j - 1])
                P.op("scalar", "activation", sgq[:, :cw], wv_, ACT.Sign)
                P.wait("scalar", tagn)
                P.op("scalar", "activation", ca[:, :cw], a_ap, ACT.Sign,
                     bias=thr7[:, 0:1])
                tc = P.op("scalar", "activation", cb[:, :cw], a_ap, ACT.Sign,
                          bias=thr7[:, 1:2], sem=s_ws)
                P.wait("vector", tc)
                tv = P.op("vector", "tensor_tensor", t1q[:, :cw], ca[:, :cw],
                          cb[:, :cw], ALU.add, sem=s_wv)
                P.wait("scalar", tv)
                P.op("scalar", "activation", ca[:, :cw], a_ap, ACT.Sign,
                     bias=thr7[:, 2:3])
                tc = P.op("scalar", "activation", cb[:, :cw], a_ap, ACT.Sign,
                          bias=thr7[:, 3:4], sem=s_ws)
                P.wait("vector", tc)
                P.op("vector", "tensor_tensor", t2q[:, :cw], ca[:, :cw],
                     cb[:, :cw], ALU.add)
                tv = P.op("vector", "tensor_tensor", t1q[:, :cw], t1q[:, :cw],
                          t2q[:, :cw], ALU.add, sem=s_wv)
                P.wait("scalar", tv)
                P.op("scalar", "activation", ca[:, :cw], a_ap, ACT.Sign,
                     bias=thr7[:, 4:5])
                tc = P.op("scalar", "activation", cb[:, :cw], a_ap, ACT.Sign,
                          bias=thr7[:, 5:6], sem=s_ws)
                P.wait("vector", tc)
                tv = P.op("vector", "tensor_tensor", t2q[:, :cw], ca[:, :cw],
                          cb[:, :cw], ALU.add, sem=s_wv)
                P.wait("scalar", tv)
                tc = P.op("scalar", "activation", ca[:, :cw], a_ap, ACT.Sign,
                          bias=thr7[:, 6:7], sem=s_ws)
                P.wait("vector", tc)
                P.op("vector", "tensor_scalar", t3q[:, :cw], ca[:, :cw],
                     2.0, 6.0, ALU.mult, ALU.add)
                P.op("vector", "tensor_tensor", t2q[:, :cw], t2q[:, :cw],
                     t3q[:, :cw], ALU.add)
                P.op("vector", "scalar_tensor_tensor", t1q[:, :cw], t1q[:, :cw],
                     0.5, t2q[:, :cw], ALU.mult, ALU.add)
                t13 = t1q[:, :cw].rearrange("p (b k) -> p b k", k=QB)
                t33 = t3q[:, :cw].rearrange("p (b k) -> p b k", k=QB)
                P.op("vector", "tensor_tensor", t33, t13, sh_b, ALU.mult)
                if j >= 1:
                    P.wait("vector", wstore_tags[j - 1])
                done = P.op("vector", "tensor_tensor", wqo[:, :cw], t3q[:, :cw],
                            sgq[:, :cw], ALU.mult, sem=s_wv)
                P.wait("sync", done)
                P.op("sync", "dma_start", sem=s_wst, k=16,
                     out=wdst[mat][ch * 128:(ch + 1) * 128, c0:c0 + cw],
                     in_=wqo[:, :cw])
                wstore_tags[j] = s_wst.cur()
                wdone_tags[j] = done
                return done

            njobs = len(wjobs)
            jobs_per_tile = (njobs + NTILE - 1) // NTILE
            wst = {"load": 0, "quant": 0}

            def emit_wprep_slice(last=False):
                target = njobs if last else min(
                    njobs, wst["quant"] + jobs_per_tile)
                while wst["quant"] < target:
                    while wst["load"] < njobs and wst["load"] < wst["quant"] + 2:
                        emit_wload(wst["load"])
                        wst["load"] += 1
                    emit_wquant(wst["quant"])
                    wst["quant"] += 1

            x_r = x_d.rearrange("(t u p) d -> t u p d", u=2, p=128)
            gated_r = gated_s.rearrange("(t u p) d -> t u p d", u=2, p=128)
            P.wait("tensor", w_cast_done)
            P.wait("tensor", consts_done)
            P.wait("tensor", t_zero)
            P.wait("vector", consts_done)
            P.wait("scalar", consts_done)

            prev = {}
            for t in range(NTILE):
                cbi = t % 2
                if t > 0:
                    P.wait("gpsimd", prev["xbf_free"])
                for u in range(2):
                    P.op("gpsimd", "dma_start", sem=s_g, k=16,
                         out=x_bf[:, u * D:(u + 1) * D], in_=x_r[t, u])
                t_xbf = s_g.cur()
                if DEBUG_TAP and t == 0:
                    P.wait("sync", t_xbf)
                    P.op("sync", "dma_start", sem=s_st, k=16,
                         out=dbg_d[:, 0:2 * D], in_=x_bf[:])
                    P.wait("vector", s_st.cur())
                if t > 1:
                    P.wait("sync", prev[("rope_used", cbi)])
                P.op("sync", "dma_start", sem=s_rp, k=16,
                     out=csf32[:, (2 * cbi) * CT:(2 * cbi + 1) * CT],
                     in_=cosF_d[:, t * CT:(t + 1) * CT])
                t_rope_l = P.op("sync", "dma_start", sem=s_rp, k=16,
                                out=csf32[:, (2 * cbi + 1) * CT:(2 * cbi + 2) * CT],
                                in_=sinF_d[:, t * CT:(t + 1) * CT])

                P.wait("tensor", t_xbf)
                for grp in range(2):
                    if t > 0 or grp > 0:
                        P.wait("tensor", prev["pt_free"])
                    for b in range(8):
                        u, fc = b % 2, b // 2 + grp * 4
                        P.op("tensor", "transpose",
                             pt_a[:, b * 128:(b + 1) * 128],
                             x_bf[:, u * D + fc * 128:u * D + fc * 128 + 128],
                             ident[:])
                    tg = P.mark("tensor", s_t)
                    P.wait("vector", tg)
                    for b in range(8):
                        u, fc = b % 2, b // 2 + grp * 4
                        P.op("vector", "tensor_copy",
                             xT[:, fc * CT + u * 128:fc * CT + u * 128 + 128],
                             pt_a[:, b * 128:(b + 1) * 128])
                    prev["pt_free"] = P.mark("vector", s_v)
                t_xT = prev["pt_free"]
                prev["xbf_free"] = t_xT

                P.wait("tensor", t_xT)
                pp = [pp_a, pp_b]
                ppfree = prev.get("ppfree", [None, None])

                def mmgroup(lhs_list, rhs_list, pb):
                    nmm = len(lhs_list)
                    for i, (l, r) in enumerate(zip(lhs_list, rhs_list)):
                        P.op("tensor", "matmul", pb, l, r,
                             start=(i == 0), stop=(i == nmm - 1))
                    return P.mark("tensor", s_t)

                pi = 0

                def proj_fm(wsb, dst):
                    nonlocal pi
                    for mq in range(8):
                        pb = pp[pi % 2]
                        if ppfree[pi % 2] is not None:
                            P.wait("tensor", ppfree[pi % 2])
                        tg = mmgroup(
                            [wsb[:, fc * D + mq * 128:fc * D + mq * 128 + 128]
                             for fc in range(8)],
                            [xT[:, fc * CT:(fc + 1) * CT] for fc in range(8)],
                            pb[:, 0:CT])
                        P.wait("vector", tg)
                        P.op("vector", "tensor_copy",
                             dst[:, mq * CT:(mq + 1) * CT], pb[:, 0:CT])
                        ppfree[pi % 2] = P.mark("vector", s_v)
                        pi += 1

                def proj_tm(wsb, dst, silu):
                    nonlocal pi
                    for u in range(2):
                        for sl in range(4):
                            pb = pp[pi % 2]
                            if ppfree[pi % 2] is not None:
                                P.wait("tensor", ppfree[pi % 2])
                            tg = mmgroup(
                                [xT[:, fc * CT + u * 128:
                                    fc * CT + u * 128 + 128]
                                 for fc in range(8)],
                                [wsb[:, fc * DVT + sl * 512:
                                     fc * DVT + (sl + 1) * 512]
                                 for fc in range(8)],
                                pb[:])
                            d = dst[:, u * DVT + sl * 512:
                                    u * DVT + (sl + 1) * 512]
                            if silu:
                                P.wait("scalar", tg)
                                P.op("scalar", "activation", d, pb[:], ACT.Silu)
                                ppfree[pi % 2] = P.mark("scalar", s_sc)
                            else:
                                P.wait("vector", tg)
                                P.op("vector", "tensor_copy", d, pb[:])
                                ppfree[pi % 2] = P.mark("vector", s_v)
                            pi += 1

                if t > 0:
                    P.wait("vector", prev["ret_done"])
                proj_fm(w_q, q_fm)
                proj_fm(w_k, k_fm)
                proj_tm(w_v, v_tm, False)
                if t > 0:
                    P.wait("scalar", prev["gated_done"])
                proj_tm(w_g, g_si, True)
                prev["ppfree"] = ppfree

                P.wait("vector", t_rope_l)
                P.op("vector", "tensor_copy", cosb[:, cbi * CT:(cbi + 1) * CT],
                     csf32[:, (2 * cbi) * CT:(2 * cbi + 1) * CT])
                P.op("vector", "tensor_copy", sinb[:, cbi * CT:(cbi + 1) * CT],
                     csf32[:, (2 * cbi + 1) * CT:(2 * cbi + 2) * CT])
                P.wait("vector", P.mark("vector", s_v))
                c_ap = cosb[:, cbi * CT:(cbi + 1) * CT]
                s_ap = sinb[:, cbi * CT:(cbi + 1) * CT]
                for h in range(NH):
                    xq1 = q_fm[:, (2 * h) * CT:(2 * h + 1) * CT]
                    xq2 = q_fm[:, (2 * h + 1) * CT:(2 * h + 2) * CT]
                    xk1 = k_fm[:, (2 * h) * CT:(2 * h + 1) * CT]
                    xk2 = k_fm[:, (2 * h + 1) * CT:(2 * h + 2) * CT]
                    P.op("vector", "tensor_tensor", u1[:], xq1, c_ap, ALU.mult)
                    P.op("vector", "tensor_tensor", u4[:], xk1, c_ap, ALU.mult)
                    P.op("vector", "tensor_tensor", u2[:], xq1, s_ap, ALU.mult)
                    P.op("vector", "tensor_tensor", u5[:], xk1, s_ap, ALU.mult)
                    P.op("vector", "tensor_tensor", u3[:], xq2, s_ap, ALU.mult)
                    P.op("vector", "tensor_tensor", u6[:], xk2, s_ap, ALU.mult)
                    P.op("vector", "tensor_tensor", xq1, u1[:], u3[:],
                         ALU.subtract)
                    P.op("vector", "tensor_tensor", xk1, u4[:], u6[:],
                         ALU.subtract)
                    P.op("vector", "tensor_tensor", u3[:], xq2, c_ap, ALU.mult)
                    P.op("vector", "tensor_tensor", u6[:], xk2, c_ap, ALU.mult)
                    P.op("vector", "tensor_tensor", xq2, u3[:], u2[:], ALU.add)
                    P.op("vector", "tensor_tensor", xk2, u6[:], u5[:], ALU.add)
                t_roped = P.mark("vector", s_v)
                prev[("rope_used", cbi)] = t_roped

                P.wait("tensor", t_roped)
                for grp in range(2):
                    P.wait("tensor", prev["pt_free"])
                    for b in range(8):
                        dc, u = b % 4 + grp * 4, b // 4
                        P.op("tensor", "transpose",
                             pt_a[:, b * 128:(b + 1) * 128],
                             k_fm[:, dc * CT + u * 128:dc * CT + u * 128 + 128],
                             ident[:])
                    tg = P.mark("tensor", s_t)
                    P.wait("vector", tg)
                    for b in range(8):
                        dc, u = b % 4 + grp * 4, b // 4
                        h = dc // 2
                        P.op("vector", "tensor_scalar",
                             kck[:, u * D + dc * 128:u * D + dc * 128 + 128],
                             pt_a[:, b * 128:(b + 1) * 128],
                             ks_pp[:, u * NH + h:u * NH + h + 1], None,
                             ALU.mult)
                    prev["pt_free"] = P.mark("vector", s_v)
                t_kck = prev["pt_free"]

                o_pp = [p_o0, p_o1]
                t_ss = {}
                t_oevac = {}
                for h in range(NH):
                    hb = h % 2
                    if h > 0:
                        P.wait("tensor", prev["mT_free"])
                    for jc in range(2):
                        o0 = jc * 256
                        n = 256 - jc * 128
                        for i, dc in enumerate((2 * h, 2 * h + 1)):
                            P.op("tensor", "matmul",
                                 p_sc[:, o0:o0 + n],
                                 k_fm[:, dc * CT + jc * 128:
                                      dc * CT + jc * 128 + 128],
                                 q_fm[:, dc * CT + jc * 128:dc * CT + 256],
                                 start=(i == 0), stop=(i == 1))
                    t_sco = P.mark("tensor", s_t)
                    P.wait("vector", t_sco)
                    P.op("vector", "tensor_tensor", mT[:, 0:128],
                         p_sc[:, 0:128],
                         dmask[:, (2 * h) * 128:(2 * h) * 128 + 128], ALU.mult)
                    P.op("vector", "tensor_tensor", mT[:, 128:256],
                         p_sc[:, 128:256],
                         dmask[:, (2 * h + 1) * 128:(2 * h + 1) * 128 + 128],
                         ALU.mult)
                    P.op("vector", "tensor_tensor", mT[:, 256:384],
                         p_sc[:, 256:384],
                         dmask[:, (2 * h) * 128:(2 * h) * 128 + 128], ALU.mult)
                    for dc in (2 * h, 2 * h + 1):
                        P.op("vector", "tensor_tensor",
                             q_fm[:, dc * CT:(dc + 1) * CT],
                             q_fm[:, dc * CT:(dc + 1) * CT],
                             qsF[:, h * CT:(h + 1) * CT], ALU.mult)
                    for dc in range(2):
                        P.op("vector", "tensor_copy",
                             Sg[:, dc * DV:(dc + 1) * DV],
                             S_st[:, (h * 2 + dc) * DV:(h * 2 + dc + 1) * DV])
                    t_mev = P.mark("vector", s_v)
                    prev["mT_free"] = t_mev

                    P.wait("tensor", t_mev)
                    if prev.get(("o_free", hb)) is not None:
                        P.wait("tensor", prev[("o_free", hb)])
                    vh = [v_tm[:, u * DVT + h * DV:u * DVT + (h + 1) * DV]
                          for u in range(2)]
                    po = o_pp[hb]
                    for ic in range(2):
                        ops = []
                        if ic == 0:
                            ops.append((mT[:, 0:128], vh[0]))
                        else:
                            ops.append((mT[:, 128:256], vh[1]))
                            ops.append((mT[:, 256:384], vh[1]))
                        for dc in (2 * h, 2 * h + 1):
                            ops.append((q_fm[:, dc * CT + ic * 128:
                                             dc * CT + ic * 128 + 128],
                                        Sg[:, (dc % 2) * DV:(dc % 2 + 1) * DV]))
                        if ic == 1:
                            ops[0] = (mT[:, 128:256], vh[0])
                        for i, (l, r) in enumerate(ops):
                            P.op("tensor", "matmul", po[:], l, r,
                                 start=(i == 0), stop=(i == len(ops) - 1))
                        t_omm = P.mark("tensor", s_t)
                        P.wait("scalar", t_omm)
                        col = h * 2 + ic
                        t_ss[col] = P.op("scalar", "activation", sqj[:], po[:],
                                         ACT.Square,
                                         accum_out=rs_s[:, col:col + 1],
                                         sem=s_sc)
                        P.wait("vector", t_ss[col])
                        if h == 0 and ic == 0 and t > 0:
                            P.wait("vector", prev["osb_free"])
                        t_oevac[col] = P.op(
                            "vector", "tensor_copy",
                            o_sb[:, ic * DVT + h * DV:ic * DVT + (h + 1) * DV],
                            po[:], sem=s_v)
                        prev[("o_free", hb)] = t_oevac[col]
                        if ic == 0:
                            P.wait("tensor", t_oevac[col])

                    P.wait("tensor", t_kck)
                    if h > 0:
                        P.wait("tensor", prev["sup_done"])
                    for dc in range(2):
                        psd = (p_s0, p_s1)[dc]
                        for u in range(2):
                            P.op("tensor", "matmul", psd[:],
                                 kck[:, u * D + (h * 2 + dc) * 128:
                                     u * D + (h * 2 + dc) * 128 + 128],
                                 vh[u], start=(u == 0), stop=(u == 1))
                    t_sd = P.mark("tensor", s_t)
                    P.wait("vector", t_sd)
                    for dc in range(2):
                        psd = (p_s0, p_s1)[dc]
                        P.op("vector", "scalar_tensor_tensor",
                             S_st[:, (h * 2 + dc) * DV:(h * 2 + dc + 1) * DV],
                             S_st[:, (h * 2 + dc) * DV:(h * 2 + dc + 1) * DV],
                             gammaC[h], psd[:], ALU.mult, ALU.add)
                    prev["sup_done"] = P.mark("vector", s_v)

                if DEBUG_TAP and t == 0:
                    for col in range(8):
                        P.wait("sync", t_ss[col])
                    P.wait("sync", t_oevac[7])
                    P.op("sync", "dma_start", sem=s_st, k=16,
                         out=dbgf_d[:, 0:D], in_=o_sb[:, 0:D].bitcast(F32)
                         ) if False else None
                    P.op("sync", "dma_start", sem=s_st, k=16,
                         out=dbg_d[:, 0:D], in_=o_sb[:, 0:D])
                    P.op("gpsimd", "dma_start", sem=s_st, k=16,
                         out=dbgf_d[:, 2 * D:2 * D + 8], in_=rs_s[:])
                    P.wait("vector", s_st.cur())
                for col in range(8):
                    P.wait("vector", t_ss[col])
                P.op("vector", "tensor_scalar", rs_w[:], rs_s[:],
                     1.0 / DV, EPS, ALU.mult, ALU.add)
                emit_rsqrt(P, s_v, rs_w[:], rs_y[:], rs_t[:], rs_lad[:], 8)
                t_rr = P.mark("vector", s_v)
                if DEBUG_TAP and t == 0:
                    P.wait("gpsimd", t_rr)
                    P.op("gpsimd", "dma_start", sem=s_st, k=16,
                         out=dbgf_d[:, 2 * D + 8:2 * D + 16], in_=rs_w[:])
                    P.op("gpsimd", "dma_start", sem=s_st, k=16,
                         out=dbgf_d[:, 2 * D + 16:2 * D + 24], in_=rs_y[:])
                    P.wait("vector", s_st.cur())
                P.wait("vector", t_rr)
                P.wait("vector", s_sc.cur())
                for h in range(NH):
                    for ic in range(2):
                        col = h * 2 + ic
                        sl = slice(ic * DVT + h * DV, ic * DVT + (h + 1) * DV)
                        P.op("vector", "scalar_tensor_tensor",
                             o_sb[:, sl], o_sb[:, sl],
                             rs_y[:, col:col + 1], gnr[:], ALU.mult, ALU.mult)
                        P.op("vector", "tensor_tensor", o_sb[:, sl],
                             o_sb[:, sl], g_si[:, sl], ALU.mult)
                t_gated = P.mark("vector", s_v)
                prev["gated_done"] = t_gated
                prev["ret_done"] = t_gated
                if DEBUG_TAP and t == 0:
                    P.wait("sync", t_gated)
                    P.op("sync", "dma_start", sem=s_st, k=16,
                         out=dbg_d[:, 2 * D:3 * D], in_=g_si[:, 0:D])
                    P.op("sync", "dma_start", sem=s_st, k=16,
                         out=dbg_d[:, 3 * D:4 * D], in_=q_fm[:, 0:D])
                    P.op("sync", "dma_start", sem=s_st, k=16,
                         out=dbg_d[:, 4 * D:4 * D + 384], in_=mT[:])
                    P.op("gpsimd", "dma_start", sem=s_st, k=16,
                         out=dbg_d[:, 4 * D + 384:4 * D + 392], in_=rs_s[:])
                    P.op("gpsimd", "dma_start", sem=s_st, k=16,
                         out=dbg_d[:, 4 * D + 392:4 * D + 400], in_=rs_y[:])
                    P.op("sync", "dma_start", sem=s_st, k=16,
                         out=dbg_d[:, 4 * D + 400:4 * D + 528], in_=ident[:])
                    P.op("sync", "dma_start", sem=s_st, k=16,
                         out=dbg_d[:, 5 * D:6 * D], in_=dmask[:])
                    P.op("sync", "dma_start", sem=s_st, k=16,
                         out=dbg_d[:, 6 * D:7 * D], in_=v_tm[:, 0:D])
                    P.op("sync", "dma_start", sem=s_st, k=16,
                         out=dbg_d[:, 7 * D:8 * D], in_=o_sb[:, 0:D])
                    P.wait("vector", s_st.cur())

                P.wait("sync", t_gated)
                for u in range(2):
                    P.op("sync", "dma_start", sem=s_st, k=16,
                         out=gated_r[t, u],
                         in_=o_sb[:, u * DVT:(u + 1) * DVT])
                prev["osb_free"] = s_st.cur()

                emit_wprep_slice(last=(t == NTILE - 1))

            a1_done_store = s_st.cur()
            P.wait("sync", a1_done_store)
            P.wait("sync", s_wst.cur())

        barrier()

        with ExitStack() as e2:
            def sb2(n, s, d):
                return e2.enter_context(nc.sbuf_tensor(n, s, d))

            def ps2(n, s, d=F32):
                return e2.enter_context(nc.psum_tensor(n, s, d))

            w_o = sb2("w_o", [128, 16 * D], BF16)
            wcast(w_o, wo_d, DVT)
            wo_done = s_wc.cur()
            P.wait("tensor", wo_done)

            gfm = sb2("gfm", [128, 16 * CT], BF16)
            xf32 = sb2("xf32", [128, 2 * D], F32)
            r1 = sb2("r1", [128, 2 * D], F32)
            x2t = sb2("x2t", [128, 2 * D], BF16)
            sqj2 = sb2("sqj2", [128, D], BF16)
            lsum = sb2("lsum", [128, 8], F32)
            lw = sb2("lw", [128, 2], F32)
            ly = sb2("ly", [128, 2], F32)
            lt = sb2("lt", [128, 2], F32)
            lm = sb2("lm", [128, 2], F32)
            lb = sb2("lb", [128, 2], F32)
            llad = sb2("llad", [128, 2 * 16], F32)
            p2a = ps2("p2a", [128, 512])
            p2b = ps2("p2b", [128, 512])

            s_x2 = sem("s_x2")
            s_gl = sem("s_gl")

            x2_r = x2_s.rearrange("(t u p) d -> t u p d", u=2, p=128)
            prev2 = {}
            for t in range(NTILE):
                if t > 0:
                    P.wait("sync", prev2["gfm_free"])
                for dvc in range(16):
                    P.op("sync", "dma_start_transpose",
                         gfm[:, dvc * CT:(dvc + 1) * CT],
                         gated_s[t * CT:(t + 1) * CT, dvc * 128:(dvc + 1) * 128],
                         sem=s_gl, k=16)
                t_g = s_gl.cur()
                if t > 0:
                    P.wait("scalar", prev2["xf_free"])
                for u in range(2):
                    P.op("scalar", "dma_start", sem=s_x2, k=16,
                         out=xf32[:, u * D:(u + 1) * D], in_=x_r[t, u])
                t_x = s_x2.cur()
                P.wait("tensor", t_g)
                pi2 = 0
                for u in range(2):
                    for s2 in range(2):
                        pb = (p2a, p2b)[pi2 % 2]
                        if prev2.get(("pfree", pi2 % 2)) is not None:
                            P.wait("tensor", prev2[("pfree", pi2 % 2)])
                        for dvc in range(16):
                            P.op("tensor", "matmul", pb[:],
                                 gfm[:, dvc * CT + u * 128:
                                     dvc * CT + u * 128 + 128],
                                 w_o[:, dvc * D + s2 * 512:
                                     dvc * D + (s2 + 1) * 512],
                                 start=(dvc == 0), stop=(dvc == 15))
                        tg = P.mark("tensor", s_t)
                        P.wait("vector", tg)
                        P.wait("vector", t_x)
                        col = u * 2 + s2
                        sl = slice(u * D + s2 * 512, u * D + (s2 + 1) * 512)
                        P.op("vector", "scalar_tensor_tensor", r1[:, sl],
                             pb[:], 1.0, xf32[:, sl], ALU.mult, ALU.add,
                             accum_out=lsum[:, col:col + 1])
                        prev2[("pfree", pi2 % 2)] = P.mark("vector", s_v)
                        pi2 += 1
                t_r1 = prev2[("pfree", (pi2 - 1) % 2)]
                prev2["gfm_free"] = t_r1
                prev2["xf_free"] = t_r1
                P.wait("scalar", t_r1)
                tss = []
                for u in range(2):
                    tss.append(P.op("scalar", "activation", sqj2[:],
                                    r1[:, u * D:(u + 1) * D], ACT.Square,
                                    accum_out=lsum[:, 4 + u:5 + u], sem=s_sc))
                P.wait("vector", tss[-1])
                def vf2():
                    P.wait("vector", P.mark("vector", s_v))
                for u in range(2):
                    P.op("vector", "tensor_tensor", lm[:, u:u + 1],
                         lsum[:, 2 * u:2 * u + 1], lsum[:, 2 * u + 1:2 * u + 2],
                         ALU.add)
                vf2()
                P.op("vector", "tensor_scalar", lm[:], lm[:], 1.0 / D, None,
                     ALU.mult)
                vf2()
                P.op("vector", "tensor_tensor", lt[:], lm[:], lm[:], ALU.mult)
                P.op("vector", "tensor_scalar", lw[:], lsum[:, 4:6], 1.0 / D,
                     None, ALU.mult)
                vf2()
                P.op("vector", "tensor_tensor", lw[:], lw[:], lt[:],
                     ALU.subtract)
                vf2()
                P.op("vector", "tensor_scalar", lw[:], lw[:], 1.0, EPS,
                     ALU.mult, ALU.add)
                emit_rsqrt(P, s_v, lw[:], ly[:], lt[:], llad[:], 2)
                P.op("vector", "tensor_tensor", lb[:], lm[:], ly[:], ALU.mult)
                vf2()
                P.op("vector", "tensor_scalar", lb[:], lb[:], -1.0, None,
                     ALU.mult)
                t_ln = P.mark("vector", s_v)
                P.wait("scalar", t_ln)
                for u in range(2):
                    P.op("scalar", "activation", x2t[:, u * D:(u + 1) * D],
                         r1[:, u * D:(u + 1) * D], ACT.Identity,
                         bias=lb[:, u:u + 1], scale=ly[:, u:u + 1])
                t_x2 = P.mark("scalar", s_sc)
                P.wait("sync", t_x2)
                for u in range(2):
                    P.op("sync", "dma_start", sem=s_st, k=16,
                         out=x2_r[t, u], in_=x2t[:, u * D:(u + 1) * D])
                prev2["x2t_free"] = s_st.cur()
                P.wait("scalar", prev2["x2t_free"])
                P.wait("vector", prev2["x2t_free"])

            P.wait("sync", s_st.cur())

        barrier()

        with ExitStack() as e3:
            def sb3(n, s, d):
                return e3.enter_context(nc.sbuf_tensor(n, s, d))

            def ps3(n, s, d=F32):
                return e3.enter_context(nc.psum_tensor(n, s, d))

            wgq = sb3("wgq", [128, 8 * IFN], BF16)
            wuq = sb3("wuq", [128, 8 * IFN], BF16)
            wdq = sb3("wdq", [128, 22 * D], BF16)
            x2b = sb3("x2b", [128, 2 * D], BF16)
            xq = sb3("xq", [128, 2 * D], BF16)
            xq_fm = sb3("xq_fm", [128, 8 * CT], BF16)
            h_tm = sb3("h_tm", [128, 2 * IFN], BF16)
            hq_fm = sb3("hq_fm", [128, 22 * CT], BF16)
            gsil = sb3("gsil", [128, 512], BF16)
            qn = sb3("qn", [128, IFN], BF16)
            qt1 = sb3("qt1", [128, IFN], BF16)
            qt2 = sb3("qt2", [128, IFN], BF16)
            qsg = sb3("qsg", [128, IFN], BF16)
            qam = sb3("qam", [128, IFN // QB], F32)
            qinv = sb3("qinv", [128, IFN // QB], F32)
            qsh = sb3("qsh", [128, IFN // QB], F32)
            r2 = sb3("r2", [128, 2 * D], F32)
            sqj3 = sb3("sqj3", [128, D], BF16)
            lsum3 = sb3("lsum3", [128, 8], F32)
            lw3 = sb3("lw3", [128, 2], F32)
            ly3 = sb3("ly3", [128, 2], F32)
            lt3 = sb3("lt3", [128, 2], F32)
            lm3 = sb3("lm3", [128, 2], F32)
            lb3 = sb3("lb3", [128, 2], F32)
            llad3 = sb3("llad3", [128, 2 * 16], F32)

            pt3 = ps3("pt3", [128, 1024], BF16)
            p_g = ps3("p_g", [128, 512])
            p_u = ps3("p_u", [128, 512])
            p_g2 = ps3("p_g2", [128, 512])
            p_u2 = ps3("p_u2", [128, 512])
            p_d0 = ps3("p_d0", [128, 512])
            p_d1 = ps3("p_d1", [128, 512])

            s_x3 = sem("s_x3")
            s_wq3 = sem("s_wq3")

            for fc in range(8):
                P.op("sync", "dma_start_transpose",
                     wgq[:, fc * IFN:(fc + 1) * IFN],
                     wgq_s[0:IFN, fc * 128:(fc + 1) * 128], sem=s_wq3, k=16)
                P.op("sync", "dma_start_transpose",
                     wuq[:, fc * IFN:(fc + 1) * IFN],
                     wuq_s[0:IFN, fc * 128:(fc + 1) * 128], sem=s_wq3, k=16)
            for ic in range(22):
                P.op("sync", "dma_start_transpose",
                     wdq[:, ic * D:(ic + 1) * D],
                     wdq_s[0:D, ic * 128:(ic + 1) * 128], sem=s_wq3, k=16)
            t_wq = s_wq3.cur()
            P.wait("tensor", t_wq)

            ISL = [(0, 512), (512, 512), (1024, 512), (1536, 512),
                   (2048, 512), (2560, 256)]

            def quant_act(src_ap, dst_ap, nb, sg_tag_eng="scalar"):
                cw = nb * QB
                s3 = src_ap.rearrange("p (b k) -> p b k", k=QB)
                def qvf():
                    P.wait("vector", P.mark("vector", s_v))
                P.op("vector", "reduce_max", qam[:, :nb], s3, AX.X, True)
                qvf()
                P.op("vector", "tensor_scalar", qinv[:, :nb], qam[:, :nb],
                     0.1666666716337204, None, ALU.mult)
                qvf()
                P.op("vector", "scalar_tensor_tensor", qinv[:, :nb],
                     qam[:, :nb], -4.967053740534411e-09, qinv[:, :nb], ALU.mult, ALU.add)
                qvf()
                P.op("vector", "tensor_scalar", qinv[:, :nb], qinv[:, :nb],
                     1.0, 1e-12, ALU.mult, ALU.max)
                qvf()
                P.op("vector", "tensor_scalar", qsh[:, :nb], qinv[:, :nb],
                     0.5, None, ALU.mult)
                qvf()
                P.op("vector", "reciprocal", qinv[:, :nb], qinv[:, :nb])
                qvf()
                inv_b = qinv[:, :nb].unsqueeze(2).broadcast_to([128, nb, QB])
                sh_b = qsh[:, :nb].unsqueeze(2).broadcast_to([128, nb, QB])
                qn3 = qn[:, :cw].rearrange("p (b k) -> p b k", k=QB)
                P.op("vector", "tensor_tensor", qn3, s3, inv_b, ALU.mult)
                P.op("vector", "scalar_tensor_tensor", qn[:, :cw], qn[:, :cw],
                     -1.0, qn[:, :cw], ALU.mult, ALU.max)
                a = qn[:, :cw]
                P.op("vector", "tensor_single_scalar", qt1[:, :cw], a, 0.25,
                     ALU.is_gt)
                P.op("vector", "scalar_tensor_tensor", qt1[:, :cw], a, 0.75,
                     qt1[:, :cw], ALU.is_gt, ALU.add)
                P.op("vector", "scalar_tensor_tensor", qt1[:, :cw], a, 1.25,
                     qt1[:, :cw], ALU.is_gt, ALU.add)
                P.op("vector", "scalar_tensor_tensor", qt1[:, :cw], a, 1.75,
                     qt1[:, :cw], ALU.is_gt, ALU.add)
                P.op("vector", "tensor_scalar", qt2[:, :cw], a, 5.0, 2.0,
                     ALU.is_gt, ALU.mult)
                P.op("vector", "scalar_tensor_tensor", qt2[:, :cw], a, 2.5,
                     qt2[:, :cw], ALU.is_gt, ALU.add)
                P.op("vector", "scalar_tensor_tensor", qt2[:, :cw], a, 3.5,
                     qt2[:, :cw], ALU.is_gt, ALU.add)
                P.op("vector", "scalar_tensor_tensor", qt1[:, :cw], qt2[:, :cw],
                     2.0, qt1[:, :cw], ALU.mult, ALU.add)
                q13 = qt1[:, :cw].rearrange("p (b k) -> p b k", k=QB)
                q23 = qt2[:, :cw].rearrange("p (b k) -> p b k", k=QB)
                P.op("vector", "tensor_tensor", q23, q13, sh_b, ALU.mult)
                P.op("vector", "tensor_tensor", dst_ap, qt2[:, :cw],
                     qsg[:, :cw], ALU.mult)

            prev3 = {}
            for t in range(NTILE):
                if t > 0:
                    P.wait("sync", prev3["x2b_free"])
                for u in range(2):
                    P.op("sync", "dma_start", sem=s_x3, k=16,
                         out=x2b[:, u * D:(u + 1) * D], in_=x2_r[t, u])
                t_x2 = s_x3.cur()
                P.wait("scalar", t_x2)
                t_sg = P.op("scalar", "activation", qsg[:, :2 * D], x2b[:],
                            ACT.Sign, sem=s_sc)
                P.wait("vector", t_x2)
                if t > 0:
                    P.wait("vector", prev3["xq_free"])
                P.wait("vector", t_sg)
                quant_act(x2b[:], xq[:], (2 * D) // QB)
                t_xq = P.mark("vector", s_v)
                P.wait("tensor", t_xq)
                for grp in range(2):
                    if prev3.get("pt_free") is not None:
                        P.wait("tensor", prev3["pt_free"])
                    for b in range(8):
                        u, fc = b % 2, b // 2 + grp * 4
                        P.op("tensor", "transpose",
                             pt3[:, b * 128:(b + 1) * 128],
                             xq[:, u * D + fc * 128:u * D + fc * 128 + 128],
                             ident[:])
                    tg = P.mark("tensor", s_t)
                    P.wait("vector", tg)
                    for b in range(8):
                        u, fc = b % 2, b // 2 + grp * 4
                        P.op("vector", "tensor_copy",
                             xq_fm[:, fc * CT + u * 128:
                                   fc * CT + u * 128 + 128],
                             pt3[:, b * 128:(b + 1) * 128])
                    prev3["pt_free"] = P.mark("vector", s_v)
                t_xqfm = prev3["pt_free"]
                prev3["xq_free"] = t_xqfm

                P.wait("tensor", t_xqfm)
                gu = 0
                for u in range(2):
                    if t > 0 and u == 0:
                        P.wait("vector", prev3["h_free"])
                    for (c0, cwid) in ISL:
                        pg = (p_g, p_g2)[gu % 2]
                        pu = (p_u, p_u2)[gu % 2]
                        if prev3.get(("gufree", gu % 2)) is not None:
                            P.wait("tensor", prev3[("gufree", gu % 2)])
                        for fc in range(8):
                            P.op("tensor", "matmul", pg[:, 0:cwid],
                                 xq_fm[:, fc * CT + u * 128:
                                       fc * CT + u * 128 + 128],
                                 wgq[:, fc * IFN + c0:fc * IFN + c0 + cwid],
                                 start=(fc == 0), stop=(fc == 7))
                        for fc in range(8):
                            P.op("tensor", "matmul", pu[:, 0:cwid],
                                 xq_fm[:, fc * CT + u * 128:
                                       fc * CT + u * 128 + 128],
                                 wuq[:, fc * IFN + c0:fc * IFN + c0 + cwid],
                                 start=(fc == 0), stop=(fc == 7))
                        tg = P.mark("tensor", s_t)
                        P.wait("scalar", tg)
                        P.op("scalar", "activation", gsil[:, 0:cwid],
                             pg[:, 0:cwid], ACT.Silu)
                        tsc = P.mark("scalar", s_sc)
                        P.wait("vector", tsc)
                        P.op("vector", "tensor_tensor",
                             h_tm[:, u * IFN + c0:u * IFN + c0 + cwid],
                             gsil[:, 0:cwid], pu[:, 0:cwid], ALU.mult)
                        prev3[("gufree", gu % 2)] = P.mark("vector", s_v)
                        gu += 1

                t_h_done = prev3[("gufree", (gu - 1) % 2)]
                P.wait("scalar", t_h_done)
                t_sg2 = P.op("scalar", "activation", qsg[:, :IFN],
                             h_tm[:, 0:IFN], ACT.Sign, sem=s_sc)
                P.wait("vector", t_sg2)
                quant_act(h_tm[:, 0:IFN], h_tm[:, 0:IFN], IFN // QB)
                t_hq0 = P.mark("vector", s_v)
                P.wait("tensor", t_hq0)
                bdone = 0
                for grp in ((0, 8), (8, 8), (16, 6)):
                    g0, gn_ = grp
                    P.wait("tensor", prev3["pt_free"])
                    for b in range(gn_):
                        ic = g0 + b
                        P.op("tensor", "transpose",
                             pt3[:, b * 128:(b + 1) * 128],
                             h_tm[:, ic * 128:(ic + 1) * 128], ident[:])
                    tg = P.mark("tensor", s_t)
                    P.wait("vector", tg)
                    for b in range(gn_):
                        ic = g0 + b
                        P.op("vector", "tensor_copy",
                             hq_fm[:, ic * CT:ic * CT + 128],
                             pt3[:, b * 128:(b + 1) * 128])
                    prev3["pt_free"] = P.mark("vector", s_v)
                P.wait("scalar", t_hq0)
                t_sg1 = P.op("scalar", "activation", qsg[:, :IFN],
                             h_tm[:, IFN:2 * IFN], ACT.Sign, sem=s_sc)
                P.wait("vector", t_sg1)
                quant_act(h_tm[:, IFN:2 * IFN], h_tm[:, IFN:2 * IFN],
                          IFN // QB)
                t_hq1 = P.mark("vector", s_v)
                P.wait("tensor", t_hq1)
                for grp in ((0, 8), (8, 8), (16, 6)):
                    g0, gn_ = grp
                    P.wait("tensor", prev3["pt_free"])
                    for b in range(gn_):
                        ic = g0 + b
                        P.op("tensor", "transpose",
                             pt3[:, b * 128:(b + 1) * 128],
                             h_tm[:, IFN + ic * 128:IFN + (ic + 1) * 128],
                             ident[:])
                    tg = P.mark("tensor", s_t)
                    P.wait("vector", tg)
                    for b in range(gn_):
                        ic = g0 + b
                        P.op("vector", "tensor_copy",
                             hq_fm[:, ic * CT + 128:ic * CT + 256],
                             pt3[:, b * 128:(b + 1) * 128])
                    prev3["pt_free"] = P.mark("vector", s_v)
                t_hqfm = prev3["pt_free"]
                prev3["h_free"] = t_hqfm

                P.wait("tensor", t_hqfm)
                pd_i = 0
                for u in range(2):
                    for s2 in range(2):
                        pd = (p_d0, p_d1)[pd_i % 2]
                        if prev3.get(("pdfree", pd_i % 2)) is not None:
                            P.wait("tensor", prev3[("pdfree", pd_i % 2)])
                        for ic in range(22):
                            P.op("tensor", "matmul", pd[:],
                                 hq_fm[:, ic * CT + u * 128:
                                       ic * CT + u * 128 + 128],
                                 wdq[:, ic * D + s2 * 512:
                                     ic * D + (s2 + 1) * 512],
                                 start=(ic == 0), stop=(ic == 21))
                        tg = P.mark("tensor", s_t)
                        P.wait("vector", tg)
                        col = u * 2 + s2
                        sl = slice(u * D + s2 * 512, u * D + (s2 + 1) * 512)
                        P.op("vector", "scalar_tensor_tensor", r2[:, sl],
                             pd[:], 1.0, x2b[:, sl], ALU.mult, ALU.add,
                             accum_out=lsum3[:, col:col + 1])
                        prev3[("pdfree", pd_i % 2)] = P.mark("vector", s_v)
                        pd_i += 1
                t_r2 = prev3[("pdfree", (pd_i - 1) % 2)]
                prev3["x2b_free"] = t_r2
                P.wait("scalar", t_r2)
                tss = []
                for u in range(2):
                    tss.append(P.op("scalar", "activation", sqj3[:],
                                    r2[:, u * D:(u + 1) * D], ACT.Square,
                                    accum_out=lsum3[:, 4 + u:5 + u], sem=s_sc))
                P.wait("vector", tss[-1])
                def vf3():
                    P.wait("vector", P.mark("vector", s_v))
                for u in range(2):
                    P.op("vector", "tensor_tensor", lm3[:, u:u + 1],
                         lsum3[:, 2 * u:2 * u + 1],
                         lsum3[:, 2 * u + 1:2 * u + 2], ALU.add)
                vf3()
                P.op("vector", "tensor_scalar", lm3[:], lm3[:], 1.0 / D, None,
                     ALU.mult)
                vf3()
                P.op("vector", "tensor_tensor", lt3[:], lm3[:], lm3[:],
                     ALU.mult)
                P.op("vector", "tensor_scalar", lw3[:], lsum3[:, 4:6],
                     1.0 / D, None, ALU.mult)
                vf3()
                P.op("vector", "tensor_tensor", lw3[:], lw3[:], lt3[:],
                     ALU.subtract)
                vf3()
                P.op("vector", "tensor_scalar", lw3[:], lw3[:], 1.0, EPS,
                     ALU.mult, ALU.add)
                emit_rsqrt(P, s_v, lw3[:], ly3[:], lt3[:], llad3[:], 2)
                P.op("vector", "tensor_tensor", lb3[:], lm3[:], ly3[:],
                     ALU.mult)
                vf3()
                P.op("vector", "tensor_scalar", lb3[:], lb3[:], -1.0, None,
                     ALU.mult)
                t_ln2 = P.mark("vector", s_v)
                P.wait("scalar", t_ln2)
                if t > 0:
                    P.wait("scalar", prev3["out_free"])
                for u in range(2):
                    P.op("scalar", "activation", r2[:, u * D:(u + 1) * D],
                         r2[:, u * D:(u + 1) * D], ACT.Identity,
                         bias=lb3[:, u:u + 1], scale=ly3[:, u:u + 1])
                t_out = P.mark("scalar", s_sc)
                P.wait("sync", t_out)
                for u in range(2):
                    P.op("sync", "dma_start", sem=s_st, k=16,
                         out=out_d.rearrange("(t u p) d -> t u p d", u=2,
                                             p=128)[t, u],
                         in_=r2[:, u * D:(u + 1) * D])
                prev3["out_free"] = s_st.cur()

            P.wait("sync", s_st.cur())

        with nc.Block() as block:
            @block.sync
            def _(eng):
                P.replay("sync", eng)

            @block.tensor
            def _(eng):
                P.replay("tensor", eng)

            @block.vector
            def _(eng):
                P.replay("vector", eng)

            @block.scalar
            def _(eng):
                P.replay("scalar", eng)

            @block.gpsimd
            def _(eng):
                P.replay("gpsimd", eng)

    return nc


def kernel(**inputs):
    x = np.ascontiguousarray(np.asarray(inputs["L0_attn_x"], np.float32))
    gn = np.asarray(inputs["L0_attn_gn_weight"], np.float32)
    gnr = np.ascontiguousarray(np.broadcast_to(gn[None, :], (128, DV)))
    common = {
        "Wq": np.ascontiguousarray(np.asarray(inputs["L0_attn_Wq"], np.float32)),
        "Wk": np.ascontiguousarray(np.asarray(inputs["L0_attn_Wk"], np.float32)),
        "Wv": np.ascontiguousarray(np.asarray(inputs["L0_attn_Wv"], np.float32)),
        "Wg": np.ascontiguousarray(np.asarray(inputs["L0_attn_Wg"], np.float32)),
        "Wo": np.ascontiguousarray(np.asarray(inputs["L0_attn_Wo"], np.float32)),
        "gnr": gnr,
        "fwg": np.ascontiguousarray(np.asarray(inputs["L0_ffn_w_gate"], np.float32)),
        "fwu": np.ascontiguousarray(np.asarray(inputs["L0_ffn_w_up"], np.float32)),
        "fwd": np.ascontiguousarray(np.asarray(inputs["L0_ffn_w_down"], np.float32)),
    }
    nc = build_nc()
    in_maps = [dict(common, x=np.ascontiguousarray(x[b]))
               for b in range(NCORES)]
    res = run_bass_kernel_spmd(nc, in_maps, core_ids=list(range(NCORES)))
    out = np.stack([np.asarray(res.results[b]["out"], np.float32)
                    for b in range(NCORES)])
    return out
